# version 25
# baseline (speedup 1.0000x reference)
import os
import numpy as np
import ml_dtypes
import concourse.bass as bass
import concourse.mybir as mybir
from concourse.bass_utils import run_bass_kernel_spmd
from contextlib import ExitStack

F32 = mybir.dt.float32
BF16 = mybir.dt.bfloat16
AF = mybir.ActivationFunctionType
ALU = mybir.AluOpType
AX = mybir.AxisListType
BFNP = ml_dtypes.bfloat16

B, T, C = 4, 4096, 1024
H, D, NLAT, PG = 16, 64, 8, 16
NCH = 8
TQ = 512
NQ = 4
SK = 512
NSK = 8
KWID = 4224
VCH = 1040
NVC = 33
EPS = 1e-5
VELT = float(2 ** 20 + 1)
FP8MAX = 448.0
ROPE_THETA = 10000.0

CB_PMAT = 0
CB_ID = 128
CB_MASK = 256
CB_ONE_U = 512
CB_ONE_L = 513
CB_ONEROW = 514
CB_BIAS = 642
NCBF = 1666
CF_ONES = 0
NCF = 128

TB_PRE = 0
TB_MK = 64
TB_KQ = 96
TB_PROD = 104
TB_MU = 112
TB_MUF = 120


def build_nc():
    nc = bass.Bass("TRN2", target_bir_lowering=False, detect_race_conditions=False)

    def din(name, shape, dt=F32):
        return nc.dram_tensor(name, list(shape), dt, kind="ExternalInput")

    xt_kv_d = din("xt_kv", (C, T))
    xt_q_d = din("xt_q", (C, 2048))
    x_res_d = din("x_res", (2048, C))
    wq_d = din("wq", (C, C))
    wk_d = din("wk", (C, C))
    wv_d = din("wv", (C, C))
    wout_d = din("wout", (C, C))
    wffn_d = din("wffn", (C, C))
    wlk_d = din("wlk", (D, C))
    wlv_d = din("wlv", (D, C))
    latk_d = din("latk", (D, NLAT))
    latv_d = din("latv", (D, NLAT))
    bias_d = din("bias", (1, C))
    scl_d = din("scl", (1, 2))
    cos_kv_d = din("cos_kv", (128, T))
    sin_kv_d = din("sin_kv", (128, T))
    cos_q_d = din("cos_q", (128, 2048))
    sin_q_d = din("sin_q", (128, 2048))
    cbf_d = din("cbf", (128, NCBF), BF16)
    cf32_d = din("cf32", (128, NCF))
    out_d = nc.dram_tensor("out", [2048, C], F32, kind="ExternalOutput")
    dbg = os.environ.get("KDBG", "") == "1"
    if dbg:
        dbg_kc = nc.dram_tensor("dbg_kc", [128, NCH * KWID], F32, kind="ExternalOutput")
        dbg_vc = nc.dram_tensor("dbg_vc", [128, NVC * VCH], F32, kind="ExternalOutput")
        dbg_q = nc.dram_tensor("dbg_q", [128, NCH * 512], F32, kind="ExternalOutput")
        dbg_ao = nc.dram_tensor("dbg_ao", [128, NCH * 512], F32, kind="ExternalOutput")
        dbg_tab = nc.dram_tensor("dbg_tab", [2, 128], F32, kind="ExternalOutput")

    es = ExitStack()
    def sbuf(name, shape, dt):
        return es.enter_context(nc.sbuf_tensor(name, list(shape), dt))
    def psum(name, shape, dt=F32):
        return es.enter_context(nc.psum_tensor(name, list(shape), dt))

    kc = sbuf("kc_s", (128, NCH * KWID), BF16)
    vc = sbuf("vc_s", (128, NVC * VCH), BF16)
    wA = sbuf("wA_s", (128, 8192), BF16)
    wB = sbuf("wB_s", (128, 8192), BF16)
    xt = sbuf("xt_s", (128, NCH * 512), BF16)
    qsb = sbuf("qsb_s", (128, NCH * 512), BF16)
    ep = sbuf("ep_s", (128, 2048), BF16)
    qraw = sbuf("qraw_s", (128, 1024), BF16)
    trig = sbuf("trig_s", (128, 1024), F32)
    xres = sbuf("xres_s", (128, 1024), F32)
    rsb = sbuf("rsb_s", (128, 1024), F32)
    hsb = sbuf("hsb_s", (128, 1024), F32)
    cbf = sbuf("cbf_s", (128, NCBF), BF16)
    cf32 = sbuf("cf32_s", (128, NCF), F32)
    srep = sbuf("srep_s", (128, 18), F32)
    lnst = sbuf("lnst_s", (128, 16), F32)
    tabs = sbuf("tabs_s", (2, 128), F32)
    latkv = sbuf("latkv_s", (128, 16), BF16)

    big1 = psum("big1", (128, 1024))
    big2 = psum("big2", (128, 1024))
    pv = psum("pv", (128, 512))
    misc = psum("misc", (128, 512))
    small = psum("small", (128, 512))
    tp = psum("tp", (128, 1024), BF16)

    ao = xt
    hq = qsb[:, 0:1024]
    hqT = qsb[:, 1024:2048]
    rcp_f32 = qraw[:].bitcast(F32)
    osb = ep[:].bitcast(F32)
    dsb0 = trig[:, 0:512]
    dsb1 = trig[:, 512:1024]

    ops = []

    class Res:
        __slots__ = ("w", "r")
        def __init__(self):
            self.w = None
            self.r = []

    def op(eng, build, reads=(), writes=(), dma=False, late=False):
        idx = len(ops)
        deps = set()
        for r in reads:
            if r.w is not None:
                deps.add(r.w)
        for w in writes:
            if w.w is not None:
                deps.add(w.w)
            deps.update(w.r)
        ops.append(dict(eng=eng, build=build, deps=deps, dma=dma, late=late,
                        need_inc=False, sem=None, val=None))
        for r in reads:
            r.r.append(idx)
        for w in writes:
            w.w = idx
            w.r = []
        return idx

    R = {}
    for name in ("kc", "vc", "wA", "wB", "xt", "qsb", "ep0", "ep1", "ep1b",
                 "qraw0", "qraw1", "trigc", "trigs", "xres", "rsb", "hsb",
                 "cbf", "cf32", "srep", "lnst", "tabs", "latkv",
                 "big1a", "big1b", "big2a", "big2b", "pv", "misc", "small",
                 "tp", "biasrep", "biasrep2"):
        R[name] = Res()
    QRAW_ALL = (R["qraw0"], R["qraw1"])
    EP_ALL = (R["ep0"], R["ep1"])

    tpf = tp[:].bitcast(F32)

    def emit_warm_tp(n=1):
        for _ in range(n):
            op("tensor", lambda e: e.matmul(
                tpf[:, 0:512], cbf[0:1, CB_ONEROW:CB_ONEROW + 128], cbf[0:1, 0:512],
                start=True, stop=True, skip_group_check=True),
               reads=[R["cbf"]], writes=[R["tp"]])

    def emit_warm(n=1):
        for _ in range(n):
            op("tensor", lambda e: e.matmul(
                pv[:, 0:512], cbf[0:1, CB_ONEROW:CB_ONEROW + 128], cbf[0:1, 0:512],
                start=True, stop=True, skip_group_check=True),
               reads=[R["cbf"]], writes=[R["pv"]])

    op("sync", lambda e: e.dma_start(cbf[:], cbf_d[:]), writes=[R["cbf"]], dma=True)
    op("sync", lambda e: e.dma_start(cf32[:], cf32_d[:]), writes=[R["cf32"]], dma=True)
    op("sync", lambda e: e.dma_start(trig[0:1, 0:2], scl_d[:]), writes=[R["trigc"]], dma=True)
    op("sync", lambda e: e.dma_start(xres[0:1, :], bias_d[:]), writes=[R["xres"]], dma=True)
    op("gpsimd", lambda e: e.dma_start(latkv[0:64, 0:8], latk_d[:]), writes=[R["latkv"]], dma=True)
    op("gpsimd", lambda e: e.dma_start(latkv[0:64, 8:16], latv_d[:]), writes=[R["latkv"]], dma=True)
    op("gpsimd", lambda e: e.dma_start(wA[0:64, 0:1024], wlk_d[:]), writes=[R["wA"]], dma=True)
    op("gpsimd", lambda e: e.dma_start(wA[0:64, 1024:2048], wlv_d[:]), writes=[R["wA"]], dma=True)
    op("gpsimd", lambda e: e.dma_start(wB[:].rearrange("p (a d) -> p a d", a=8), wv_d[:].rearrange("(a p) d -> p a d", p=128)),
       writes=[R["wB"]], dma=True)

    op("vector", lambda e: e.memset(big1[:, 0:512], 0.0), writes=[R["big1a"]])
    op("vector", lambda e: e.memset(big1[:, 512:1024], 0.0), writes=[R["big1b"]])
    op("vector", lambda e: e.memset(big2[:, 0:512], 0.0), writes=[R["big2a"]])
    op("vector", lambda e: e.memset(big2[:, 512:1024], 0.0), writes=[R["big2b"]])
    op("vector", lambda e: e.memset(pv[:], 0.0), writes=[R["pv"]])
    op("vector", lambda e: e.memset(misc[:], 0.0), writes=[R["misc"]])
    op("vector", lambda e: e.memset(small[:], 0.0), writes=[R["small"]])
    op("vector", lambda e: e.memset(tp[:].bitcast(F32), 0.0), writes=[R["tp"]])
    op("vector", lambda e: e.memset(vc[:].rearrange("p (a w) -> p a w", a=NVC), 1.0),
       writes=[R["vc"]])
    op("vector", lambda e: e.memset(vc[:, 32 * VCH:33 * VCH], 0.0), writes=[R["vc"]])
    op("vector", lambda e: e.memset(
        bass.AP(vc, 32 * VCH + 64, [[NVC * VCH, 8], [65, 16]]), 1.0), writes=[R["vc"]])
    op("vector",
       lambda e: e.memset(kc[:].rearrange("p (a w) -> p a w", a=NCH)[:, :, 4096:KWID], 0.0),
       writes=[R["kc"]])
    op("vector", lambda e: e.memset(tabs[:], 0.0), writes=[R["tabs"]])

    op("vector", lambda e: e.reciprocal(trig[0:1, 2:3], trig[0:1, 0:1]),
       reads=[R["trigc"]], writes=[R["trigc"]], late=True)
    op("vector", lambda e: e.tensor_mul(trig[0:1, 3:4], trig[0:1, 0:1], trig[0:1, 1:2]),
       reads=[R["trigc"]], writes=[R["trigc"]], late=True)
    op("vector", lambda e: e.reciprocal(trig[0:1, 4:5], trig[0:1, 3:4]),
       reads=[R["trigc"]], writes=[R["trigc"]], late=True)
    op("tensor", lambda e: e.matmul(small[:, 0:1], cf32[0:1, 0:128], trig[0:1, 2:3],
                                    start=True, stop=True, skip_group_check=True),
       reads=[R["cf32"], R["trigc"]], writes=[R["small"]])
    op("tensor", lambda e: e.matmul(small[:, 1:2], cf32[0:1, 0:128], trig[0:1, 3:4],
                                    start=True, stop=True, skip_group_check=True),
       reads=[R["cf32"], R["trigc"]], writes=[R["small"]])
    op("vector", lambda e: e.tensor_copy(srep[:, 0:2], small[:, 0:2]),
       reads=[R["small"]], writes=[R["srep"]], late=True)
    op("vector", lambda e: e.tensor_scalar_mul(cbf[0:1, CB_BIAS:CB_BIAS + 1024],
                                               xres[0:1, :], trig[0:1, 4:5]),
       reads=[R["xres"], R["trigc"]], writes=[R["cbf"]])

    for j in range(NCH):
        op("tensor", lambda e, j=j: e.matmul(misc[:, 0:8], wA[0:64, 128 * j:128 * (j + 1)],
                                             latkv[0:64, 0:8], start=True, stop=True,
                                             skip_group_check=True),
           reads=[R["wA"], R["latkv"]], writes=[R["misc"]])
        op("vector", lambda e, j=j: e.tensor_copy(kc[:, j * KWID + 4096: j * KWID + 4104],
                                                  misc[:, 0:8]),
           reads=[R["misc"]], writes=[R["kc"]])
    for nh in range(2):
        op("tensor", lambda e, nh=nh: e.matmul(
            misc[0:8, 0:512], latkv[0:64, 8:16],
            wA[0:64, 1024 + nh * 512:1536 + nh * 512], start=True, stop=True,
            skip_group_check=True),
           reads=[R["wA"], R["latkv"]], writes=[R["misc"]])
        op("vector", lambda e, nh=nh: e.tensor_copy(
            vc[0:8, 32 * VCH + nh * 520: 32 * VCH + nh * 520 + 520]
              .rearrange("p (a w) -> p a w", a=8)[:, :, 0:64],
            misc[0:8, 0:512].rearrange("p (a w) -> p a w", a=8)),
           reads=[R["misc"]], writes=[R["vc"]])
    op("gpsimd", lambda e: e.dma_start(wA[:].rearrange("p (a d) -> p a d", a=8), wk_d[:].rearrange("(a p) d -> p a d", p=128)),
       reads=[R["wA"]], writes=[R["wA"]], dma=True)
    emit_warm(12)

    def emit_xt_dma(src_d, col0, ncols, xbuf=None, xres_r=None):
        if xbuf is None:
            xbuf, xres_r = xt, R["xt"]
        ap = src_d[:].rearrange("(a p) t -> p a t", p=128)[:, :, col0:col0 + ncols]
        op("gpsimd", lambda e, ap=ap, ncols=ncols, xbuf=xbuf: e.dma_start(
            xbuf[:].rearrange("p (a t) -> p a t", a=NCH)[:, :, 0:ncols], ap),
           reads=[xres_r], writes=[xres_r], dma=True)

    def emit_trig_dma(cd, sd, col0):
        op("sync", lambda e, cd=cd, col0=col0: e.dma_start(trig[:, 0:512], cd[:, col0:col0 + 512]),
           reads=[R["trigc"]], writes=[R["trigc"]], dma=True)
        op("sync", lambda e, sd=sd, col0=col0: e.dma_start(trig[:, 512:1024], sd[:, col0:col0 + 512]),
           reads=[R["trigs"]], writes=[R["trigs"]], dma=True)

    RAWSLOT = [(big1[:, 0:512], R["big1a"]), (big1[:, 512:1024], R["big1b"]),
               (big2[:, 0:512], R["big2a"]), (big2[:, 512:1024], R["big2b"])]


    def emit_proj(j, wbuf, wres, xbuf, xres_r):
        raw, braw = RAWSLOT[j % 4]
        for cc in range(NCH):
            op("tensor", lambda e, cc=cc, raw=raw, j=j, wbuf=wbuf, xbuf=xbuf: e.matmul(
                raw, wbuf[:, cc * 1024 + 128 * j: cc * 1024 + 128 * (j + 1)],
                xbuf[:, cc * 512: cc * 512 + 512],
                start=(cc == 0), stop=(cc == NCH - 1), skip_group_check=True),
               reads=[wres, xres_r], writes=[braw])
        par = j % 2
        op("scalar", lambda e, raw=raw, par=par: e.activation(
            qraw[:, par * 512: par * 512 + 512], raw, AF.Copy),
           reads=[braw], writes=[R["qraw0"] if par == 0 else R["qraw1"]])

    def emit_swap_mm(j):
        par = j % 2
        qres = R["qraw0"] if par == 0 else R["qraw1"]
        qrw = qraw[:, par * 512: par * 512 + 512]
        op("tensor", lambda e, qrw=qrw: e.matmul(
            misc[:, 0:512], cbf[:, CB_PMAT:CB_PMAT + 128], qrw, start=True, stop=True,
            skip_group_check=True),
           reads=[R["cbf"], qres], writes=[R["misc"]])

    def emit_rope_tail(j, dst, dres, kq_col):
        raw, braw = RAWSLOT[j % 4]
        t1 = ep[:, 0:512]
        t2 = ep[:, 512:1024]
        sq = ep[:, 1024 + (j % 2) * 512: 1536 + (j % 2) * 512]
        sqres = R["ep1"] if j % 2 == 0 else R["ep1b"]
        op("vector", lambda e, t1=t1: e.tensor_mul(t1, misc[:, 0:512], trig[:, 512:1024]),
           reads=[R["misc"], R["trigs"]], writes=[R["ep0"]])
        op("vector", lambda e, t2=t2, raw=raw: e.tensor_mul(t2, raw, trig[:, 0:512]),
           reads=[braw, R["trigc"]], writes=[R["ep0"]])
        op("vector", lambda e, dst=dst, t1=t1, t2=t2: e.tensor_add(dst, t1, t2),
           reads=[R["ep0"]], writes=[dres])
        op("vector", lambda e, sq=sq, dst=dst: e.tensor_mul(sq, dst, dst),
           reads=[dres], writes=[sqres])

    def emit_norm_tail(j, kq_col):
        sq = ep[:, 1024 + (j % 2) * 512: 1536 + (j % 2) * 512]
        sqres = R["ep1"] if j % 2 == 0 else R["ep1b"]
        op("tensor", lambda e, sq=sq: e.matmul(
            small[0:2, 0:512], cbf[:, CB_ONE_U:CB_ONE_U + 2], sq,
            start=True, stop=True, skip_group_check=True),
           reads=[R["cbf"], sqres], writes=[R["small"]])
        op("vector", lambda e, kq_col=kq_col: e.reduce_max(
            tabs[0:2, kq_col:kq_col + 1], small[0:2, 0:512], AX.X),
           reads=[R["small"]], writes=[R["tabs"]], late=True)


    def xbuf_of(tau):
        return (xt, R["xt"]) if tau % 2 == 0 else (qsb, R["qsb"])

    for tau in range(NSK):
        if tau == 0:
            emit_xt_dma(xt_kv_d, 0, SK, *xbuf_of(0))
        emit_trig_dma(cos_kv_d, sin_kv_d, tau * SK)
        emit_warm(1)
        xb, xr = xbuf_of(tau)
        pend = []
        for j in range(NCH):
            emit_proj(j, wA, R["wA"], xb, xr)
            if j == 1 and tau + 1 < NSK:
                emit_xt_dma(xt_kv_d, (tau + 1) * SK, SK, *xbuf_of(tau + 1))
            if pend:
                emit_swap_mm(pend[-1][0])
                emit_rope_tail(*pend[-1])
            if len(pend) >= 2:
                emit_norm_tail(pend[0][0], pend[0][3])
                pend.pop(0)
            pend.append((j, kc[:, j * KWID + tau * SK: j * KWID + tau * SK + 512],
                         R["kc"], TB_PRE + j * 8 + tau))
        emit_swap_mm(pend[-1][0])
        emit_rope_tail(*pend[-1])
        for p in pend:
            emit_norm_tail(p[0], p[3])

    for k in range(NQ):
        op("vector", lambda e, k=k: e.reduce_max(
            tabs[0:2, TB_MK + 8 * k: TB_MK + 8 * k + 8],
            tabs[0:2, TB_PRE:TB_PRE + 64].rearrange("p (a b) -> p a b", a=8)[:, :, 0:2 * k + 2],
            AX.X),
           reads=[R["tabs"]], writes=[R["tabs"]], late=True)

    op("gpsimd", lambda e: e.dma_start(wA[:].rearrange("p (a d) -> p a d", a=8), wq_d[:].rearrange("(a p) d -> p a d", p=128)),
       reads=[R["wA"]], writes=[R["wA"]], dma=True)

    for tau in range(NSK):
        if tau == 0:
            emit_xt_dma(xt_kv_d, 0, SK, *xbuf_of(0))
        xb, xr = xbuf_of(tau)
        vdefer = []
        for ssub in range(4):
            schunk = 4 * tau + ssub
            for nh in range(2):
                par = (ssub + nh) % 2
                bres = R["big1a"] if par == 0 else R["big1b"]
                dst_ps = big1[:, par * 512: par * 512 + 512]
                for cc in range(NCH):
                    op("tensor", lambda e, cc=cc, dst_ps=dst_ps, ssub=ssub, nh=nh,
                       xb=xb: e.matmul(
                        dst_ps, xb[:, cc * 512 + ssub * 128: cc * 512 + ssub * 128 + 128],
                        wB[:, cc * 1024 + nh * 512: cc * 1024 + nh * 512 + 512],
                        start=(cc == 0), stop=(cc == NCH - 1), skip_group_check=True),
                       reads=[xr, R["wB"]], writes=[bres])
                if ssub == 0 and nh == 0 and tau + 1 < NSK:
                    emit_xt_dma(xt_kv_d, (tau + 1) * SK, SK, *xbuf_of(tau + 1))
                op("vector", lambda e, schunk=schunk, nh=nh, dst_ps=dst_ps: e.tensor_copy(
                    vc[:, schunk * VCH + nh * 520: schunk * VCH + nh * 520 + 520]
                      .rearrange("p (a w) -> p a w", a=8)[:, :, 0:64],
                    dst_ps.rearrange("p (a w) -> p a w", a=8)),
                   reads=[bres], writes=[R["vc"]])

    op("gpsimd", lambda e: e.dma_start(wB[:].rearrange("p (a d) -> p a d", a=8), wout_d[:].rearrange("(a p) d -> p a d", p=128)),
       reads=[R["wB"]], writes=[R["wB"]], dma=True)

    if dbg:
        op("gpsimd", lambda e: e.dma_start(dbg_kc[:], kc[:]), reads=[R["kc"]], dma=True)
        op("gpsimd", lambda e: e.dma_start(dbg_vc[:], vc[:]), reads=[R["vc"]], dma=True)

    def emit_ln(src_res, src, dst_res_list, dst):
        op("vector", lambda e, src=src: e.bn_stats(lnst[:, 0:6], src[:, 0:512]),
           reads=[src_res], writes=[R["lnst"]], late=True)
        op("vector", lambda e, src=src: e.bn_stats(lnst[:, 6:12], src[:, 512:1024]),
           reads=[src_res], writes=[R["lnst"]], late=True)
        op("vector", lambda e: e.bn_aggr(lnst[:, 12:14], lnst[:, 0:12]),
           reads=[R["lnst"]], writes=[R["lnst"]], late=True)
        op("vector", lambda e: e.tensor_scalar_mul(lnst[:, 10:11], lnst[:, 12:13], -1.0),
           reads=[R["lnst"]], writes=[R["lnst"]], late=True)
        op("vector", lambda e: e.tensor_scalar_add(lnst[:, 11:12], lnst[:, 13:14], EPS),
           reads=[R["lnst"]], writes=[R["lnst"]], late=True)
        op("vector", lambda e: e.reciprocal(lnst[:, 14:15], lnst[:, 11:12]),
           reads=[R["lnst"]], writes=[R["lnst"]], late=True)
        op("scalar", lambda e: e.activation(lnst[:, 15:16], lnst[:, 14:15], AF.Sqrt),
           reads=[R["lnst"]], writes=[R["lnst"]])
        op("vector", lambda e, src=src, dst=dst: e.tensor_scalar(
            dst, src, lnst[:, 10:11], lnst[:, 15:16], ALU.add, ALU.mult),
           reads=[src_res, R["lnst"]], writes=list(dst_res_list))

    for k in range(NQ):
        wq_buf, wq_res = (wA, R["wA"]) if k % 2 == 0 else (wB, R["wB"])
        wo_buf, wo_res = (wB, R["wB"]) if k % 2 == 0 else (wA, R["wA"])
        emit_xt_dma(xt_q_d, k * TQ, TQ)
        emit_trig_dma(cos_q_d, sin_q_d, k * TQ)
        pend = []
        for j in range(NCH):
            emit_warm(1)
            emit_proj(j, wq_buf, wq_res, xt, R["xt"])
            if pend:
                emit_swap_mm(pend[-1][0])
                emit_rope_tail(*pend[-1])
            if len(pend) >= 2:
                emit_norm_tail(pend[0][0], pend[0][3])
                pend.pop(0)
            pend.append((j, qsb[:, j * 512: j * 512 + 512], R["qsb"], TB_KQ + j))
        emit_swap_mm(pend[-1][0])
        emit_rope_tail(*pend[-1])
        for p in pend:
            emit_norm_tail(p[0], p[3])
        op("vector", lambda e, k=k: e.tensor_mul(tabs[0:2, TB_PROD:TB_PROD + 8],
                                                 tabs[0:2, TB_KQ:TB_KQ + 8],
                                                 tabs[0:2, TB_MK + 8 * k:TB_MK + 8 * k + 8]),
           reads=[R["tabs"]], writes=[R["tabs"]], late=True)
        op("scalar", lambda e: e.activation(tabs[0:2, TB_MU:TB_MU + 8],
                                            tabs[0:2, TB_PROD:TB_PROD + 8], AF.Sqrt),
           reads=[R["tabs"]], writes=[R["tabs"]])
        op("vector", lambda e: e.tensor_scalar_mul(tabs[0:2, TB_MU:TB_MU + 8],
                                                   tabs[0:2, TB_MU:TB_MU + 8], -1.0),
           reads=[R["tabs"]], writes=[R["tabs"]], late=True)
        op("gpsimd", lambda e: e.dma_start(tabs[0:1, TB_MUF:TB_MUF + 8],
                                           tabs[1:2, TB_MU:TB_MU + 8]),
           reads=[R["tabs"]], writes=[R["tabs"]], dma=True)
        op("tensor", lambda e: e.matmul(
            misc[:, 0:8], cf32[0:1, 0:128], tabs[0:1, TB_MU:TB_MU + 8],
            start=True, stop=True, skip_group_check=True),
           reads=[R["cf32"], R["tabs"]], writes=[R["misc"]])
        op("tensor", lambda e: e.matmul(
            misc[:, 8:16], cf32[0:1, 0:128], tabs[0:1, TB_MUF:TB_MUF + 8],
            start=True, stop=True, skip_group_check=True),
           reads=[R["cf32"], R["tabs"]], writes=[R["misc"]])
        op("vector", lambda e: e.tensor_copy(srep[:, 2:18], misc[:, 0:16]),
           reads=[R["misc"]], writes=[R["biasrep"], R["biasrep2"]], late=True)
        if dbg and k == 0:
            op("gpsimd", lambda e: e.dma_start(dbg_q[:], qsb[:]), reads=[R["qsb"]], dma=True)
            op("sync", lambda e: e.dma_start(dbg_tab[:], tabs[:]), reads=[R["tabs"]], dma=True)
        op("gpsimd", lambda e, wq_buf=wq_buf: e.dma_start(wq_buf[:].rearrange("p (a d) -> p a d", a=8), wffn_d[:].rearrange("(a p) d -> p a d", p=128)),
           reads=[wq_res], writes=[wq_res], dma=True)

        nkey = 8 * k + 8
        pending_epi = []
        chunks = [(c, 0 if c < 8 * k else 64 * (c - 8 * k), c) for c in range(nkey)]
        chunks.append((nkey, 0, 32))
        groups = [chunks[i:i + 2] for i in range(0, nkey, 2)] + [chunks[nkey:]]
        for j in range(NCH):
            hA, hB = 2 * j, 2 * j + 1

            def emit_qkt(grp):
                for ci, (c, t0, vcix) in enumerate(grp):
                    kcol0 = j * KWID + (4096 if vcix == 32 else 128 * c)
                    for par, gbuf, gr in ((0, big1, (R["big1a"], R["big1b"])),
                                          (1, big2, (R["big2a"], R["big2b"]))):
                        op("tensor", lambda e, ci=ci, t0=t0, kcol0=kcol0, j=j, par=par,
                           gbuf=gbuf: e.matmul(
                            gbuf[:, ci * 512 + t0: ci * 512 + 512],
                            kc[64 * par:64 * par + 64, kcol0:kcol0 + 128],
                            qsb[64 * par:64 * par + 64, j * 512 + t0: j * 512 + 512],
                            start=True, stop=True, skip_group_check=True),
                           reads=[R["kc"], R["qsb"]], writes=[gr[ci]])

            HEADCFG = ((0, big1, (R["big1a"], R["big1b"]), R["ep0"], 0, 2 + j,
                        R["biasrep"], pv, R["pv"]),
                       (1, big2, (R["big2a"], R["big2b"]), R["ep1"], 1024, 10 + j,
                        R["biasrep2"], small, R["small"]))

            def emit_exp_h(cfg, grp):
                par, gbuf, gr, eres, ecol, bcol, brs = cfg[:7]
                wide = 512 * len(grp)
                diag = len(grp) == 2 and grp[0][0] >= 8 * k
                je = grp[0][0] - 8 * k if diag else 0
                if diag:
                    off0, off1 = 64 * je, 512 + 64 * (je + 1)
                    op("scalar", lambda e, ecol=ecol, gbuf=gbuf, bcol=bcol, off0=off0:
                       e.activation(ep[:, ecol + off0:ecol + 512], gbuf[:, off0:512],
                                    AF.Exp, bias=srep[:, bcol:bcol + 1]),
                       reads=[gr[0], brs], writes=[eres])
                    op("scalar", lambda e, ecol=ecol, gbuf=gbuf, bcol=bcol, off1=off1:
                       e.activation(ep[:, ecol + off1:ecol + 1024], gbuf[:, off1:1024],
                                    AF.Exp, bias=srep[:, bcol:bcol + 1]),
                       reads=[gr[1], brs], writes=[eres])
                else:
                    op("scalar", lambda e, ecol=ecol, wide=wide, gbuf=gbuf, bcol=bcol:
                       e.activation(ep[:, ecol:ecol + wide], gbuf[:, 0:wide], AF.Exp,
                                    bias=srep[:, bcol:bcol + 1]),
                       reads=[gr[0], gr[1], brs], writes=[eres])
                if diag:
                    off = ecol + 64 * je
                    mask_ap = bass.AP(ep, off, [[2048, 128], [512, 2], [1, 128]])
                    op("vector", lambda e, mask_ap=mask_ap: e.tensor_tensor(
                        mask_ap, mask_ap,
                        cbf[:, CB_MASK:CB_MASK + 256].rearrange("p (a w) -> p a w", a=2),
                        ALU.mult),
                       reads=[eres, R["cbf"]], writes=[eres])

            def emit_pv_h(cfg, grp):
                par, _, _, eres, ecol = cfg[:5]
                pvb, pvres = cfg[7], cfg[8]
                h = 2 * j + par
                for ci, (c, t0, vcix) in enumerate(grp):
                    op("tensor", lambda e, ci=ci, t0=t0, vcix=vcix, h=h, ecol=ecol,
                       pvb=pvb, first=(c == 0), last=(vcix == 32): e.matmul(
                        pvb[0:65, t0:512],
                        vc[:, vcix * VCH + h * 65: vcix * VCH + h * 65 + 65],
                        ep[:, ecol + ci * 512 + t0: ecol + ci * 512 + 512],
                        start=first, stop=last, skip_group_check=True),
                       reads=[R["vc"], eres], writes=[pvres])

            emit_qkt(groups[0])
            for g in range(len(groups)):
                if g > 0:
                    emit_pv_h(HEADCFG[0], groups[g - 1])
                emit_exp_h(HEADCFG[0], groups[g])
                if g > 0:
                    emit_pv_h(HEADCFG[1], groups[g - 1])
                emit_exp_h(HEADCFG[1], groups[g])
                if g + 1 < len(groups):
                    emit_qkt(groups[g + 1])
                if g == 1 and pending_epi:
                    for fn in pending_epi:
                        fn()
                    pending_epi = []
            emit_pv_h(HEADCFG[0], groups[-1])
            emit_pv_h(HEADCFG[1], groups[-1])

            for par, pvb, pvres, dslot, dres in (
                    (0, pv, R["pv"], dsb0, R["trigc"]),
                    (1, small, R["small"], dsb1, R["trigs"])):
                op("vector", lambda e, dslot=dslot, pvb=pvb: e.tensor_copy(
                    dslot[64:65, :], pvb[64:65, :]),
                   reads=[pvres], writes=[dres], late=True)
                if par == 0:
                    op("vector", lambda e, j=j, pvb=pvb: e.tensor_copy(
                        ao[0:64, j * 512: j * 512 + 512], pvb[0:64, :]),
                       reads=[pvres], writes=[R["xt"]])
                else:
                    op("vector", lambda e, pvb=pvb: e.tensor_copy(
                        qraw[0:64, 0:512], pvb[0:64, :]),
                       reads=[pvres], writes=[R["qraw0"]])
                    op("gpsimd", lambda e, j=j: e.dma_start(
                        ao[64:128, j * 512: j * 512 + 512], qraw[0:64, 0:512]),
                       reads=[R["qraw0"]], writes=[R["xt"]], dma=True)

            def _epi(j=j):
                for dslot, dres in ((dsb0, R["trigc"]), (dsb1, R["trigs"])):
                    op("scalar", lambda e, dslot=dslot: e.activation(
                        hsb[64:65, 0:512], dslot[64:65, :], AF.Ln),
                       reads=[dres], writes=[R["hsb"]])
                    op("scalar", lambda e, dslot=dslot: e.activation(
                        dslot[64:65, :], hsb[64:65, 0:512], AF.Exp, scale=-1.0),
                       reads=[R["hsb"]], writes=[dres], late=True)
                op("tensor", lambda e: e.matmul(
                    misc[0:64, 0:512], cf32[64:65, 0:64], dsb0[64:65, :],
                    start=True, stop=True, skip_group_check=True),
                   reads=[R["cf32"], R["trigc"]], writes=[R["misc"]])
                op("tensor", lambda e: e.matmul(
                    misc[64:128, 0:512], cf32[64:65, 0:64], dsb1[64:65, :],
                    start=True, stop=True, skip_group_check=True),
                   reads=[R["cf32"], R["trigs"]], writes=[R["misc"]])
                op("vector", lambda e: e.tensor_copy(rcp_f32[:, 0:512], misc[:, 0:512]),
                   reads=[R["misc"]], writes=list(QRAW_ALL))
                op("vector", lambda e, j=j: e.tensor_mul(
                    ao[:, j * 512: j * 512 + 512], ao[:, j * 512: j * 512 + 512],
                    rcp_f32[:, 0:512]),
                   reads=list(QRAW_ALL), writes=[R["xt"]])
            if j == NCH - 1:
                _epi()
            else:
                pending_epi.append(_epi)

        if dbg and k == 0:
            op("gpsimd", lambda e: e.dma_start(dbg_ao[:], ao[:]), reads=[R["xt"]], dma=True)
        op("sync", lambda e, trow=k * 512: e.dma_start(xres[:], x_res_d[trow:trow + 128, :]),
           reads=[R["xres"]], writes=[R["xres"]], dma=True)
        for tsub in range(4):
            trow = k * 512 + tsub * 128
            for nh in range(2):
                for cc in range(NCH):
                    op("tensor", lambda e, cc=cc, nh=nh, tsub=tsub, wo_buf=wo_buf: e.matmul(
                        big1[:, nh * 512: nh * 512 + 512],
                        ao[:, cc * 512 + tsub * 128: cc * 512 + tsub * 128 + 128],
                        wo_buf[:, cc * 1024 + nh * 512: cc * 1024 + nh * 512 + 512],
                        start=(cc == 0), stop=(cc == NCH - 1), skip_group_check=True),
                       reads=[R["xt"], wo_res],
                       writes=[R["big1a"] if nh == 0 else R["big1b"]])
            op("vector", lambda e: e.tensor_add(rsb[:], big1[:, 0:1024], xres[:]),
               reads=[R["big1a"], R["big1b"], R["xres"]], writes=[R["rsb"]])
            emit_warm(2)
            emit_ln(R["rsb"], rsb[:], [R["hsb"]], hsb[:])
            emit_warm(2)
            op("vector", lambda e: e.tensor_scalar(rsb[:], hsb[:], srep[:, 0:1], FP8MAX,
                                                   ALU.mult, ALU.min),
               reads=[R["hsb"], R["srep"]], writes=[R["rsb"]])
            op("vector", lambda e: e.tensor_scalar_max(rsb[:], rsb[:], -FP8MAX),
               reads=[R["rsb"]], writes=[R["rsb"]])
            op("vector", lambda e: e.tensor_scalar_mul(xres[:], rsb[:], VELT),
               reads=[R["rsb"]], writes=[R["xres"]])
            op("gpsimd", lambda e: e.tensor_sub(osb[:], xres[:], rsb[:]),
               reads=[R["xres"], R["rsb"]], writes=list(EP_ALL))
            op("vector", lambda e: e.tensor_sub(hq, xres[:], osb[:]),
               reads=[R["xres"]] + list(EP_ALL), writes=[R["qsb"]])
            if tsub < 3:
                op("sync", lambda e, trow=trow + 128: e.dma_start(
                    xres[:], x_res_d[trow:trow + 128, :]),
                   reads=[R["xres"]], writes=[R["xres"]], dma=True)
            for cc in range(NCH):
                sl = (cc % 2) * 128
                op("tensor", lambda e, cc=cc, sl=sl: e.transpose(
                    tp[:, sl:sl + 128], hq[:, cc * 128: cc * 128 + 128],
                    cbf[:, CB_ID:CB_ID + 128]),
                   reads=[R["qsb"], R["cbf"]], writes=[R["tp"]])
                op("vector", lambda e, cc=cc, sl=sl: e.tensor_copy(
                    hqT[:, cc * 128: cc * 128 + 128], tp[:, sl:sl + 128]),
                   reads=[R["tp"]], writes=[R["qsb"]])
            for nh in range(2):
                op("tensor", lambda e, nh=nh: e.matmul(
                    big2[:, nh * 512: nh * 512 + 512], cbf[0:1, CB_ONEROW:CB_ONEROW + 128],
                    cbf[0:1, CB_BIAS + nh * 512: CB_BIAS + nh * 512 + 512],
                    start=True, stop=False, skip_group_check=True),
                   reads=[R["cbf"]], writes=[R["big2a"] if nh == 0 else R["big2b"]])
                for cc in range(NCH):
                    op("tensor", lambda e, cc=cc, nh=nh, wq_buf=wq_buf: e.matmul(
                        big2[:, nh * 512: nh * 512 + 512],
                        hqT[:, cc * 128: cc * 128 + 128],
                        wq_buf[:, cc * 1024 + nh * 512: cc * 1024 + nh * 512 + 512],
                        start=False, stop=(cc == NCH - 1), skip_group_check=True),
                       reads=[R["qsb"], wq_res],
                       writes=[R["big2a"] if nh == 0 else R["big2b"]])
            op("vector", lambda e: e.scalar_tensor_tensor(
                rsb[:], big2[:, 0:1024], srep[:, 1:2], hsb[:], ALU.mult, ALU.add),
               reads=[R["big2a"], R["big2b"], R["srep"], R["hsb"]], writes=[R["rsb"]])
            emit_warm(2)
            emit_ln(R["rsb"], rsb[:], EP_ALL, osb[:])
            emit_warm(2)
            op("sync", lambda e, trow=trow: e.dma_start(out_d[trow:trow + 128, :], osb[:]),
               reads=list(EP_ALL), dma=True)

        if k < NQ - 1:
            op("gpsimd", lambda e, wo_buf=wo_buf: e.dma_start(wo_buf[:].rearrange("p (a d) -> p a d", a=8), wq_d[:].rearrange("(a p) d -> p a d", p=128)),
               reads=[wo_res], writes=[wo_res], dma=True)
            op("gpsimd", lambda e, wq_buf=wq_buf: e.dma_start(wq_buf[:].rearrange("p (a d) -> p a d", a=8), wout_d[:].rearrange("(a p) d -> p a d", p=128)),
               reads=[wq_res], writes=[wq_res], dma=True)

    ENGS = ("sync", "tensor", "scalar", "vector", "gpsimd")
    for o in ops:
        if o["dma"]:
            o["need_inc"] = True
        for d in o["deps"]:
            dd = ops[d]
            if dd["dma"] or dd["eng"] != o["eng"] or dd["late"]:
                dd["need_inc"] = True
    sem_of = {}
    for e in ENGS:
        sem_of[(e, False)] = "c_" + e
        sem_of[(e, True)] = "d_" + e
    sems = {nm: es.enter_context(nc.semaphore(nm)) for nm in set(sem_of.values())}
    counts = {nm: 0 for nm in sems}
    for o in ops:
        if o["need_inc"]:
            nm = sem_of[(o["eng"], o["dma"])]
            counts[nm] += 16 if o["dma"] else 1
            o["sem"] = nm
            o["val"] = counts[nm]
    finals = dict(counts)

    def emit_for(eng_name):
        def run(eng):
            w = {}
            for o in ops:
                if o["eng"] != eng_name:
                    continue
                for d in sorted(o["deps"]):
                    dd = ops[d]
                    if not (dd["dma"] or dd["eng"] != eng_name or dd["late"]):
                        continue
                    if dd["sem"] is None:
                        continue
                    if w.get(dd["sem"], 0) < dd["val"]:
                        eng.wait_ge(sems[dd["sem"]], dd["val"])
                        w[dd["sem"]] = dd["val"]
                inst = o["build"](eng)
                if o["need_inc"]:
                    inst.then_inc(sems[o["sem"]], 16 if o["dma"] else 1)
                if o["dma"]:
                    eng.wait_ge(sems[o["sem"]], o["val"])
                    w[o["sem"]] = max(w.get(o["sem"], 0), o["val"])
            if eng_name == "sync":
                for nm, v in finals.items():
                    if v > 0:
                        eng.wait_ge(sems[nm], v)
        return run

    with nc.Block() as blk:
        blk.sync(emit_for("sync"))
        blk.tensor(emit_for("tensor"))
        blk.scalar(emit_for("scalar"))
        blk.vector(emit_for("vector"))
        blk.gpsimd(emit_for("gpsimd"))

    return nc



def _perm():
    p = np.zeros(C, dtype=np.int64)
    for h in range(H):
        for m in range(D):
            p[h * D + m] = h * D + (2 * m if m < 32 else 2 * (m - 32) + 1)
    return p


def _rope_tables(pos, scale):
    i = np.arange(32, dtype=np.float64)
    freqs = 1.0 / (ROPE_THETA ** (2.0 * i / D))
    ang = freqs[:, None] * np.asarray(pos, np.float64)[None, :]
    cos32 = np.cos(ang) * scale
    sin32 = np.sin(ang) * scale
    cos = np.concatenate([cos32, cos32, cos32, cos32], axis=0)
    sin = np.concatenate([-sin32, sin32, -sin32, sin32], axis=0)
    return np.ascontiguousarray(cos.astype(np.float32)), np.ascontiguousarray(sin.astype(np.float32))


def _consts(pi):
    cbf = np.zeros((128, NCBF), dtype=np.float32)
    for m in range(128):
        sw = m + 32 if (m % 64) < 32 else m - 32
        cbf[sw, CB_PMAT + m] = 1.0
    cbf[:, CB_ID:CB_ID + 128] = np.eye(128, dtype=np.float32)
    u = np.arange(128)
    for q in range(256):
        half, qq = q // 128, (q % 128) // 16
        lim = 32 * qq + 16 * pi + (16 if half == 0 else -112)
        cbf[:, CB_MASK + q] = (u < lim).astype(np.float32)
    cbf[0:64, CB_ONE_U] = 1.0
    cbf[64:128, CB_ONE_L] = 1.0
    cbf[0, CB_ONEROW:CB_ONEROW + 128] = 1.0
    cf32 = np.ones((128, NCF), dtype=np.float32)
    return np.ascontiguousarray(cbf.astype(BFNP)), cf32


def kernel(L0_attn_x, L0_attn_Wq, L0_attn_Wk, L0_attn_Wv, L0_attn_lat_k,
           L0_attn_lat_v, L0_attn_W_lk, L0_attn_W_lv, L0_attn_W_out,
           L0_ffn_weight_fp8, L0_ffn_bias, L0_ffn_input_scale,
           L0_ffn_weight_scale):
    x = np.asarray(L0_attn_x, dtype=np.float32)
    perm = _perm()
    wq = np.ascontiguousarray(np.asarray(L0_attn_Wq, np.float32)[:, perm])
    wk = np.ascontiguousarray(np.asarray(L0_attn_Wk, np.float32)[:, perm])
    wv = np.ascontiguousarray(np.asarray(L0_attn_Wv, np.float32))
    wout = np.ascontiguousarray(np.asarray(L0_attn_W_out, np.float32))
    wffn = np.ascontiguousarray(np.asarray(L0_ffn_weight_fp8, np.float32).T)
    wlk = np.ascontiguousarray(np.asarray(L0_attn_W_lk, np.float32)[:, perm])
    wlv = np.ascontiguousarray(np.asarray(L0_attn_W_lv, np.float32))
    latk = np.ascontiguousarray(np.asarray(L0_attn_lat_k, np.float32)[0].T)
    latv = np.ascontiguousarray(np.asarray(L0_attn_lat_v, np.float32)[0].T)
    bias = np.ascontiguousarray(np.asarray(L0_ffn_bias, np.float32).reshape(1, C))
    scl = np.array([[float(np.asarray(L0_ffn_input_scale).reshape(-1)[0]),
                     float(np.asarray(L0_ffn_weight_scale).reshape(-1)[0])]],
                   dtype=np.float32)
    cos_kv, sin_kv = _rope_tables(np.arange(T), 1.0)

    in_maps = []
    row_sets = []
    for c in range(8):
        b, pi = c // 2, c % 2
        rows = np.arange(T)
        rows = rows[(rows // PG) % 2 == pi]
        row_sets.append(rows)
        own = np.ascontiguousarray(x[b][rows])
        cos_q, sin_q = _rope_tables(rows, 1.0 / np.sqrt(D))
        cbf_np, cf32_np = _consts(pi)
        in_maps.append({
            "xt_kv": np.ascontiguousarray(x[b].T),
            "xt_q": np.ascontiguousarray(own.T),
            "x_res": own,
            "wq": wq, "wk": wk, "wv": wv, "wout": wout, "wffn": wffn,
            "wlk": wlk, "wlv": wlv, "latk": latk, "latv": latv,
            "bias": bias, "scl": scl,
            "cos_kv": cos_kv, "sin_kv": sin_kv,
            "cos_q": cos_q, "sin_q": sin_q,
            "cbf": cbf_np, "cf32": cf32_np,
        })

    nc = build_nc()
    res = run_bass_kernel_spmd(nc, in_maps, core_ids=list(range(8)))
    out = np.empty((B, T, C), dtype=np.float32)
    for c in range(8):
        out[c // 2][row_sets[c]] = np.asarray(res.results[c]["out"], dtype=np.float32)
    return out


# version 26 (final)
# speedup vs baseline: 1.0416x; 1.0416x over previous
import os
import numpy as np
import ml_dtypes
import concourse.bass as bass
import concourse.mybir as mybir
from concourse.bass_utils import run_bass_kernel_spmd
from contextlib import ExitStack

F32 = mybir.dt.float32
BF16 = mybir.dt.bfloat16
AF = mybir.ActivationFunctionType
ALU = mybir.AluOpType
AX = mybir.AxisListType
BFNP = ml_dtypes.bfloat16

B, T, C = 4, 4096, 1024
H, D, NLAT, PG = 16, 64, 8, 16
NCH = 8
TQ = 512
NQ = 4
SK = 512
NSK = 8
KWID = 4224
VCH = 1040
NVC = 33
EPS = 1e-5
VELT = float(2 ** 20 + 1)
FP8MAX = 448.0
ROPE_THETA = 10000.0

CB_PMAT = 0
CB_ID = 128
CB_MASK = 256
CB_ONE_U = 512
CB_ONE_L = 513
CB_ONEROW = 514
CB_BIAS = 642
NCBF = 1666
CF_ONES = 0
NCF = 128

TB_PRE = 0
TB_MK = 64
TB_KQ = 96
TB_PROD = 104
TB_MU = 112
TB_MUF = 120


def build_nc():
    nc = bass.Bass("TRN2", target_bir_lowering=False, detect_race_conditions=False)

    def din(name, shape, dt=F32):
        return nc.dram_tensor(name, list(shape), dt, kind="ExternalInput")

    xt_kv_d = din("xt_kv", (C, T))
    xt_q_d = din("xt_q", (C, 2048))
    x_res_d = din("x_res", (2048, C))
    wq_d = din("wq", (C, C))
    wk_d = din("wk", (C, C))
    wv_d = din("wv", (C, C))
    wout_d = din("wout", (C, C))
    wffn_d = din("wffn", (C, C))
    wlk_d = din("wlk", (D, C))
    wlv_d = din("wlv", (D, C))
    latk_d = din("latk", (D, NLAT))
    latv_d = din("latv", (D, NLAT))
    bias_d = din("bias", (1, C))
    scl_d = din("scl", (1, 2))
    cos_kv_d = din("cos_kv", (128, T))
    sin_kv_d = din("sin_kv", (128, T))
    cos_q_d = din("cos_q", (128, 2048))
    sin_q_d = din("sin_q", (128, 2048))
    cbf_d = din("cbf", (128, NCBF), BF16)
    cf32_d = din("cf32", (128, NCF))
    out_d = nc.dram_tensor("out", [2048, C], F32, kind="ExternalOutput")
    dbg = os.environ.get("KDBG", "") == "1"
    if dbg:
        dbg_kc = nc.dram_tensor("dbg_kc", [128, NCH * KWID], F32, kind="ExternalOutput")
        dbg_vc = nc.dram_tensor("dbg_vc", [128, NVC * VCH], F32, kind="ExternalOutput")
        dbg_q = nc.dram_tensor("dbg_q", [128, NCH * 512], F32, kind="ExternalOutput")
        dbg_ao = nc.dram_tensor("dbg_ao", [128, NCH * 512], F32, kind="ExternalOutput")
        dbg_tab = nc.dram_tensor("dbg_tab", [2, 128], F32, kind="ExternalOutput")

    es = ExitStack()
    def sbuf(name, shape, dt):
        return es.enter_context(nc.sbuf_tensor(name, list(shape), dt))
    def psum(name, shape, dt=F32):
        return es.enter_context(nc.psum_tensor(name, list(shape), dt))

    kc = sbuf("kc_s", (128, NCH * KWID), BF16)
    vc = sbuf("vc_s", (128, NVC * VCH), BF16)
    wA = sbuf("wA_s", (128, 8192), BF16)
    wB = sbuf("wB_s", (128, 8192), BF16)
    xt = sbuf("xt_s", (128, NCH * 512), BF16)
    qsb = sbuf("qsb_s", (128, NCH * 512), BF16)
    ep = sbuf("ep_s", (128, 2048), BF16)
    qraw = sbuf("qraw_s", (128, 1024), BF16)
    trig = sbuf("trig_s", (128, 1024), F32)
    xres = sbuf("xres_s", (128, 1024), F32)
    rsb = sbuf("rsb_s", (128, 1024), F32)
    hsb = sbuf("hsb_s", (128, 1024), F32)
    cbf = sbuf("cbf_s", (128, NCBF), BF16)
    cf32 = sbuf("cf32_s", (128, NCF), F32)
    srep = sbuf("srep_s", (128, 18), F32)
    lnst = sbuf("lnst_s", (128, 16), F32)
    tabs = sbuf("tabs_s", (2, 128), F32)
    latkv = sbuf("latkv_s", (128, 16), BF16)

    big1 = psum("big1", (128, 1024))
    big2 = psum("big2", (128, 1024))
    pv = psum("pv", (128, 512))
    misc = psum("misc", (128, 512))
    small = psum("small", (128, 512))
    tp = psum("tp", (128, 1024), BF16)

    ao = xt
    hq = qsb[:, 0:1024]
    hqT = qsb[:, 1024:2048]
    rcp_f32 = qraw[:].bitcast(F32)
    osb = ep[:].bitcast(F32)
    dsb0 = trig[:, 0:512]
    dsb1 = trig[:, 512:1024]

    ops = []

    class Res:
        __slots__ = ("w", "r")
        def __init__(self):
            self.w = None
            self.r = []

    def op(eng, build, reads=(), writes=(), dma=False, late=False):
        idx = len(ops)
        deps = set()
        for r in reads:
            if r.w is not None:
                deps.add(r.w)
        for w in writes:
            if w.w is not None:
                deps.add(w.w)
            deps.update(w.r)
        ops.append(dict(eng=eng, build=build, deps=deps, dma=dma, late=late,
                        need_inc=False, sem=None, val=None))
        for r in reads:
            r.r.append(idx)
        for w in writes:
            w.w = idx
            w.r = []
        return idx

    R = {}
    for name in ("kc", "vc", "wA", "wB", "xt", "qsb", "ep0", "ep1", "ep1b",
                 "qraw0", "qraw1", "trigc", "trigs", "xres", "rsb", "hsb",
                 "cbf", "cf32", "srep", "lnst", "tabs", "latkv",
                 "big1a", "big1b", "big2a", "big2b", "pv", "misc", "small",
                 "tp", "biasrep", "biasrep2"):
        R[name] = Res()
    QRAW_ALL = (R["qraw0"], R["qraw1"])
    EP_ALL = (R["ep0"], R["ep1"])

    tpf = tp[:].bitcast(F32)

    def emit_warm_tp(n=1):
        for _ in range(n):
            op("tensor", lambda e: e.matmul(
                tpf[:, 0:512], cbf[0:1, CB_ONEROW:CB_ONEROW + 128], cbf[0:1, 0:512],
                start=True, stop=True, skip_group_check=True),
               reads=[R["cbf"]], writes=[R["tp"]])

    def emit_warm(n=1):
        for _ in range(n):
            op("tensor", lambda e: e.matmul(
                pv[:, 0:512], cbf[0:1, CB_ONEROW:CB_ONEROW + 128], cbf[0:1, 0:512],
                start=True, stop=True, skip_group_check=True),
               reads=[R["cbf"]], writes=[R["pv"]])

    op("sync", lambda e: e.dma_start(cbf[:], cbf_d[:]), writes=[R["cbf"]], dma=True)
    op("sync", lambda e: e.dma_start(cf32[:], cf32_d[:]), writes=[R["cf32"]], dma=True)
    op("sync", lambda e: e.dma_start(trig[0:1, 0:2], scl_d[:]), writes=[R["trigc"]], dma=True)
    op("sync", lambda e: e.dma_start(xres[0:1, :], bias_d[:]), writes=[R["xres"]], dma=True)
    op("gpsimd", lambda e: e.dma_start(latkv[0:64, 0:8], latk_d[:]), writes=[R["latkv"]], dma=True)
    op("gpsimd", lambda e: e.dma_start(latkv[0:64, 8:16], latv_d[:]), writes=[R["latkv"]], dma=True)
    op("gpsimd", lambda e: e.dma_start(wA[0:64, 0:1024], wlk_d[:]), writes=[R["wA"]], dma=True)
    op("gpsimd", lambda e: e.dma_start(wA[0:64, 1024:2048], wlv_d[:]), writes=[R["wA"]], dma=True)
    op("gpsimd", lambda e: e.dma_start(wB[:].rearrange("p (a d) -> p a d", a=8), wv_d[:].rearrange("(a p) d -> p a d", p=128)),
       writes=[R["wB"]], dma=True)

    op("vector", lambda e: e.memset(big1[:, 0:512], 0.0), writes=[R["big1a"]])
    op("vector", lambda e: e.memset(big1[:, 512:1024], 0.0), writes=[R["big1b"]])
    op("vector", lambda e: e.memset(big2[:, 0:512], 0.0), writes=[R["big2a"]])
    op("vector", lambda e: e.memset(big2[:, 512:1024], 0.0), writes=[R["big2b"]])
    op("vector", lambda e: e.memset(pv[:], 0.0), writes=[R["pv"]])
    op("vector", lambda e: e.memset(misc[:], 0.0), writes=[R["misc"]])
    op("vector", lambda e: e.memset(small[:], 0.0), writes=[R["small"]])
    op("vector", lambda e: e.memset(tp[:].bitcast(F32), 0.0), writes=[R["tp"]])
    op("vector", lambda e: e.memset(vc[:].rearrange("p (a w) -> p a w", a=NVC), 1.0),
       writes=[R["vc"]])
    op("vector", lambda e: e.memset(vc[:, 32 * VCH:33 * VCH], 0.0), writes=[R["vc"]])
    op("vector", lambda e: e.memset(
        bass.AP(vc, 32 * VCH + 64, [[NVC * VCH, 8], [65, 16]]), 1.0), writes=[R["vc"]])
    op("vector",
       lambda e: e.memset(kc[:].rearrange("p (a w) -> p a w", a=NCH)[:, :, 4096:KWID], 0.0),
       writes=[R["kc"]])
    op("vector", lambda e: e.memset(tabs[:], 0.0), writes=[R["tabs"]])

    op("vector", lambda e: e.reciprocal(trig[0:1, 2:3], trig[0:1, 0:1]),
       reads=[R["trigc"]], writes=[R["trigc"]], late=True)
    op("vector", lambda e: e.tensor_mul(trig[0:1, 3:4], trig[0:1, 0:1], trig[0:1, 1:2]),
       reads=[R["trigc"]], writes=[R["trigc"]], late=True)
    op("vector", lambda e: e.reciprocal(trig[0:1, 4:5], trig[0:1, 3:4]),
       reads=[R["trigc"]], writes=[R["trigc"]], late=True)
    op("tensor", lambda e: e.matmul(small[:, 0:1], cf32[0:1, 0:128], trig[0:1, 2:3],
                                    start=True, stop=True, skip_group_check=True),
       reads=[R["cf32"], R["trigc"]], writes=[R["small"]])
    op("tensor", lambda e: e.matmul(small[:, 1:2], cf32[0:1, 0:128], trig[0:1, 3:4],
                                    start=True, stop=True, skip_group_check=True),
       reads=[R["cf32"], R["trigc"]], writes=[R["small"]])
    op("vector", lambda e: e.tensor_copy(srep[:, 0:2], small[:, 0:2]),
       reads=[R["small"]], writes=[R["srep"]], late=True)
    op("vector", lambda e: e.tensor_scalar_mul(cbf[0:1, CB_BIAS:CB_BIAS + 1024],
                                               xres[0:1, :], trig[0:1, 4:5]),
       reads=[R["xres"], R["trigc"]], writes=[R["cbf"]])

    for j in range(NCH):
        op("tensor", lambda e, j=j: e.matmul(misc[:, 0:8], wA[0:64, 128 * j:128 * (j + 1)],
                                             latkv[0:64, 0:8], start=True, stop=True,
                                             skip_group_check=True),
           reads=[R["wA"], R["latkv"]], writes=[R["misc"]])
        op("vector", lambda e, j=j: e.tensor_copy(kc[:, j * KWID + 4096: j * KWID + 4104],
                                                  misc[:, 0:8]),
           reads=[R["misc"]], writes=[R["kc"]])
    for nh in range(2):
        op("tensor", lambda e, nh=nh: e.matmul(
            misc[0:8, 0:512], latkv[0:64, 8:16],
            wA[0:64, 1024 + nh * 512:1536 + nh * 512], start=True, stop=True,
            skip_group_check=True),
           reads=[R["wA"], R["latkv"]], writes=[R["misc"]])
        op("vector", lambda e, nh=nh: e.tensor_copy(
            vc[0:8, 32 * VCH + nh * 520: 32 * VCH + nh * 520 + 520]
              .rearrange("p (a w) -> p a w", a=8)[:, :, 0:64],
            misc[0:8, 0:512].rearrange("p (a w) -> p a w", a=8)),
           reads=[R["misc"]], writes=[R["vc"]])
    op("gpsimd", lambda e: e.dma_start(wA[:].rearrange("p (a d) -> p a d", a=8), wk_d[:].rearrange("(a p) d -> p a d", p=128)),
       reads=[R["wA"]], writes=[R["wA"]], dma=True)
    emit_warm(12)

    def emit_xt_dma(src_d, col0, ncols, xbuf=None, xres_r=None):
        if xbuf is None:
            xbuf, xres_r = xt, R["xt"]
        ap = src_d[:].rearrange("(a p) t -> p a t", p=128)[:, :, col0:col0 + ncols]
        op("gpsimd", lambda e, ap=ap, ncols=ncols, xbuf=xbuf: e.dma_start(
            xbuf[:].rearrange("p (a t) -> p a t", a=NCH)[:, :, 0:ncols], ap),
           reads=[xres_r], writes=[xres_r], dma=True)

    def emit_trig_dma(cd, sd, col0):
        op("sync", lambda e, cd=cd, col0=col0: e.dma_start(trig[:, 0:512], cd[:, col0:col0 + 512]),
           reads=[R["trigc"]], writes=[R["trigc"]], dma=True)
        op("sync", lambda e, sd=sd, col0=col0: e.dma_start(trig[:, 512:1024], sd[:, col0:col0 + 512]),
           reads=[R["trigs"]], writes=[R["trigs"]], dma=True)

    RAWSLOT = [(big1[:, 0:512], R["big1a"]), (big1[:, 512:1024], R["big1b"]),
               (big2[:, 0:512], R["big2a"]), (big2[:, 512:1024], R["big2b"])]


    def emit_proj(j, wbuf, wres, xbuf, xres_r):
        raw, braw = RAWSLOT[j % 4]
        for cc in range(NCH):
            op("tensor", lambda e, cc=cc, raw=raw, j=j, wbuf=wbuf, xbuf=xbuf: e.matmul(
                raw, wbuf[:, cc * 1024 + 128 * j: cc * 1024 + 128 * (j + 1)],
                xbuf[:, cc * 512: cc * 512 + 512],
                start=(cc == 0), stop=(cc == NCH - 1), skip_group_check=True),
               reads=[wres, xres_r], writes=[braw])
        par = j % 2
        op("scalar", lambda e, raw=raw, par=par: e.activation(
            qraw[:, par * 512: par * 512 + 512], raw, AF.Copy),
           reads=[braw], writes=[R["qraw0"] if par == 0 else R["qraw1"]])

    def emit_swap_mm(j):
        par = j % 2
        qres = R["qraw0"] if par == 0 else R["qraw1"]
        qrw = qraw[:, par * 512: par * 512 + 512]
        op("tensor", lambda e, qrw=qrw: e.matmul(
            misc[:, 0:512], cbf[:, CB_PMAT:CB_PMAT + 128], qrw, start=True, stop=True,
            skip_group_check=True),
           reads=[R["cbf"], qres], writes=[R["misc"]])

    def emit_rope_tail(j, dst, dres, kq_col):
        raw, braw = RAWSLOT[j % 4]
        t1 = ep[:, 0:512]
        t2 = ep[:, 512:1024]
        sq = ep[:, 1024 + (j % 2) * 512: 1536 + (j % 2) * 512]
        sqres = R["ep1"] if j % 2 == 0 else R["ep1b"]
        op("vector", lambda e, t1=t1: e.tensor_mul(t1, misc[:, 0:512], trig[:, 512:1024]),
           reads=[R["misc"], R["trigs"]], writes=[R["ep0"]])
        op("vector", lambda e, t2=t2, raw=raw: e.tensor_mul(t2, raw, trig[:, 0:512]),
           reads=[braw, R["trigc"]], writes=[R["ep0"]])
        op("vector", lambda e, dst=dst, t1=t1, t2=t2: e.tensor_add(dst, t1, t2),
           reads=[R["ep0"]], writes=[dres])
        op("vector", lambda e, sq=sq, dst=dst: e.tensor_mul(sq, dst, dst),
           reads=[dres], writes=[sqres])

    def emit_norm_tail(j, kq_col):
        sq = ep[:, 1024 + (j % 2) * 512: 1536 + (j % 2) * 512]
        sqres = R["ep1"] if j % 2 == 0 else R["ep1b"]
        op("tensor", lambda e, sq=sq: e.matmul(
            small[0:2, 0:512], cbf[:, CB_ONE_U:CB_ONE_U + 2], sq,
            start=True, stop=True, skip_group_check=True),
           reads=[R["cbf"], sqres], writes=[R["small"]])
        op("vector", lambda e, kq_col=kq_col: e.reduce_max(
            tabs[0:2, kq_col:kq_col + 1], small[0:2, 0:512], AX.X),
           reads=[R["small"]], writes=[R["tabs"]], late=True)


    def xbuf_of(tau):
        return (xt, R["xt"]) if tau % 2 == 0 else (qsb, R["qsb"])

    for tau in range(NSK):
        if tau == 0:
            emit_xt_dma(xt_kv_d, 0, SK, *xbuf_of(0))
        emit_trig_dma(cos_kv_d, sin_kv_d, tau * SK)
        emit_warm(1)
        xb, xr = xbuf_of(tau)
        pend = []
        for j in range(NCH):
            emit_proj(j, wA, R["wA"], xb, xr)
            if j == 1 and tau + 1 < NSK:
                emit_xt_dma(xt_kv_d, (tau + 1) * SK, SK, *xbuf_of(tau + 1))
            if pend:
                emit_swap_mm(pend[-1][0])
                emit_rope_tail(*pend[-1])
            if len(pend) >= 2:
                emit_norm_tail(pend[0][0], pend[0][3])
                pend.pop(0)
            pend.append((j, kc[:, j * KWID + tau * SK: j * KWID + tau * SK + 512],
                         R["kc"], TB_PRE + j * 8 + tau))
        emit_swap_mm(pend[-1][0])
        emit_rope_tail(*pend[-1])
        for p in pend:
            emit_norm_tail(p[0], p[3])

    for k in range(NQ):
        op("vector", lambda e, k=k: e.reduce_max(
            tabs[0:2, TB_MK + 8 * k: TB_MK + 8 * k + 8],
            tabs[0:2, TB_PRE:TB_PRE + 64].rearrange("p (a b) -> p a b", a=8)[:, :, 0:2 * k + 2],
            AX.X),
           reads=[R["tabs"]], writes=[R["tabs"]], late=True)

    op("gpsimd", lambda e: e.dma_start(wA[:].rearrange("p (a d) -> p a d", a=8), wq_d[:].rearrange("(a p) d -> p a d", p=128)),
       reads=[R["wA"]], writes=[R["wA"]], dma=True)

    for tau in range(NSK):
        if tau == 0:
            emit_xt_dma(xt_kv_d, 0, SK, *xbuf_of(0))
        xb, xr = xbuf_of(tau)
        vdefer = []
        for ssub in range(4):
            schunk = 4 * tau + ssub
            for nh in range(2):
                par = (ssub + nh) % 2
                bres = R["big1a"] if par == 0 else R["big1b"]
                dst_ps = big1[:, par * 512: par * 512 + 512]
                for cc in range(NCH):
                    op("tensor", lambda e, cc=cc, dst_ps=dst_ps, ssub=ssub, nh=nh,
                       xb=xb: e.matmul(
                        dst_ps, xb[:, cc * 512 + ssub * 128: cc * 512 + ssub * 128 + 128],
                        wB[:, cc * 1024 + nh * 512: cc * 1024 + nh * 512 + 512],
                        start=(cc == 0), stop=(cc == NCH - 1), skip_group_check=True),
                       reads=[xr, R["wB"]], writes=[bres])
                if ssub == 0 and nh == 0 and tau + 1 < NSK:
                    emit_xt_dma(xt_kv_d, (tau + 1) * SK, SK, *xbuf_of(tau + 1))
                op("vector", lambda e, schunk=schunk, nh=nh, dst_ps=dst_ps: e.tensor_copy(
                    vc[:, schunk * VCH + nh * 520: schunk * VCH + nh * 520 + 520]
                      .rearrange("p (a w) -> p a w", a=8)[:, :, 0:64],
                    dst_ps.rearrange("p (a w) -> p a w", a=8)),
                   reads=[bres], writes=[R["vc"]])

    op("gpsimd", lambda e: e.dma_start(wB[:].rearrange("p (a d) -> p a d", a=8), wout_d[:].rearrange("(a p) d -> p a d", p=128)),
       reads=[R["wB"]], writes=[R["wB"]], dma=True)

    if dbg:
        op("gpsimd", lambda e: e.dma_start(dbg_kc[:], kc[:]), reads=[R["kc"]], dma=True)
        op("gpsimd", lambda e: e.dma_start(dbg_vc[:], vc[:]), reads=[R["vc"]], dma=True)

    def emit_ln(src_res, src, dst_res_list, dst):
        op("vector", lambda e, src=src: e.bn_stats(lnst[:, 0:6], src[:, 0:512]),
           reads=[src_res], writes=[R["lnst"]], late=True)
        op("vector", lambda e, src=src: e.bn_stats(lnst[:, 6:12], src[:, 512:1024]),
           reads=[src_res], writes=[R["lnst"]], late=True)
        op("vector", lambda e: e.bn_aggr(lnst[:, 12:14], lnst[:, 0:12]),
           reads=[R["lnst"]], writes=[R["lnst"]], late=True)
        op("vector", lambda e: e.tensor_scalar_mul(lnst[:, 10:11], lnst[:, 12:13], -1.0),
           reads=[R["lnst"]], writes=[R["lnst"]], late=True)
        op("vector", lambda e: e.tensor_scalar_add(lnst[:, 11:12], lnst[:, 13:14], EPS),
           reads=[R["lnst"]], writes=[R["lnst"]], late=True)
        op("vector", lambda e: e.reciprocal(lnst[:, 14:15], lnst[:, 11:12]),
           reads=[R["lnst"]], writes=[R["lnst"]], late=True)
        op("scalar", lambda e: e.activation(lnst[:, 15:16], lnst[:, 14:15], AF.Sqrt),
           reads=[R["lnst"]], writes=[R["lnst"]])
        op("vector", lambda e, src=src, dst=dst: e.tensor_scalar(
            dst, src, lnst[:, 10:11], lnst[:, 15:16], ALU.add, ALU.mult),
           reads=[src_res, R["lnst"]], writes=list(dst_res_list))

    for k in range(NQ):
        wq_buf, wq_res = (wA, R["wA"]) if k % 2 == 0 else (wB, R["wB"])
        wo_buf, wo_res = (wB, R["wB"]) if k % 2 == 0 else (wA, R["wA"])
        emit_xt_dma(xt_q_d, k * TQ, TQ)
        emit_trig_dma(cos_q_d, sin_q_d, k * TQ)
        pend = []
        for j in range(NCH):
            emit_warm(1)
            emit_proj(j, wq_buf, wq_res, xt, R["xt"])
            if pend:
                emit_swap_mm(pend[-1][0])
                emit_rope_tail(*pend[-1])
            if len(pend) >= 2:
                emit_norm_tail(pend[0][0], pend[0][3])
                pend.pop(0)
            pend.append((j, qsb[:, j * 512: j * 512 + 512], R["qsb"], TB_KQ + j))
        emit_swap_mm(pend[-1][0])
        emit_rope_tail(*pend[-1])
        for p in pend:
            emit_norm_tail(p[0], p[3])
        op("vector", lambda e, k=k: e.tensor_mul(tabs[0:2, TB_PROD:TB_PROD + 8],
                                                 tabs[0:2, TB_KQ:TB_KQ + 8],
                                                 tabs[0:2, TB_MK + 8 * k:TB_MK + 8 * k + 8]),
           reads=[R["tabs"]], writes=[R["tabs"]], late=True)
        op("scalar", lambda e: e.activation(tabs[0:2, TB_MU:TB_MU + 8],
                                            tabs[0:2, TB_PROD:TB_PROD + 8], AF.Sqrt),
           reads=[R["tabs"]], writes=[R["tabs"]])
        op("vector", lambda e: e.tensor_scalar_mul(tabs[0:2, TB_MU:TB_MU + 8],
                                                   tabs[0:2, TB_MU:TB_MU + 8], -1.0),
           reads=[R["tabs"]], writes=[R["tabs"]], late=True)
        op("gpsimd", lambda e: e.dma_start(tabs[0:1, TB_MUF:TB_MUF + 8],
                                           tabs[1:2, TB_MU:TB_MU + 8]),
           reads=[R["tabs"]], writes=[R["tabs"]], dma=True)
        op("tensor", lambda e: e.matmul(
            misc[:, 0:8], cf32[0:1, 0:128], tabs[0:1, TB_MU:TB_MU + 8],
            start=True, stop=True, skip_group_check=True),
           reads=[R["cf32"], R["tabs"]], writes=[R["misc"]])
        op("tensor", lambda e: e.matmul(
            misc[:, 8:16], cf32[0:1, 0:128], tabs[0:1, TB_MUF:TB_MUF + 8],
            start=True, stop=True, skip_group_check=True),
           reads=[R["cf32"], R["tabs"]], writes=[R["misc"]])
        op("vector", lambda e: e.tensor_copy(srep[:, 2:18], misc[:, 0:16]),
           reads=[R["misc"]], writes=[R["biasrep"], R["biasrep2"]], late=True)
        if dbg and k == 0:
            op("gpsimd", lambda e: e.dma_start(dbg_q[:], qsb[:]), reads=[R["qsb"]], dma=True)
            op("sync", lambda e: e.dma_start(dbg_tab[:], tabs[:]), reads=[R["tabs"]], dma=True)
        op("gpsimd", lambda e, wq_buf=wq_buf: e.dma_start(wq_buf[:].rearrange("p (a d) -> p a d", a=8), wffn_d[:].rearrange("(a p) d -> p a d", p=128)),
           reads=[wq_res], writes=[wq_res], dma=True)

        nkey = 8 * k + 8
        pending_epi = []
        chunks = [(c, 0 if c < 8 * k else 64 * (c - 8 * k), c) for c in range(nkey)]
        chunks.append((nkey, 0, 32))
        groups = [chunks[i:i + 2] for i in range(0, nkey, 2)] + [chunks[nkey:]]
        for j in range(NCH):
            hA, hB = 2 * j, 2 * j + 1

            def emit_qkt(grp):
                for ci, (c, t0, vcix) in enumerate(grp):
                    kcol0 = j * KWID + (4096 if vcix == 32 else 128 * c)
                    for par, gbuf, gr in ((0, big1, (R["big1a"], R["big1b"])),
                                          (1, big2, (R["big2a"], R["big2b"]))):
                        op("tensor", lambda e, ci=ci, t0=t0, kcol0=kcol0, j=j, par=par,
                           gbuf=gbuf: e.matmul(
                            gbuf[:, ci * 512 + t0: ci * 512 + 512],
                            kc[64 * par:64 * par + 64, kcol0:kcol0 + 128],
                            qsb[64 * par:64 * par + 64, j * 512 + t0: j * 512 + 512],
                            start=True, stop=True, skip_group_check=True),
                           reads=[R["kc"], R["qsb"]], writes=[gr[ci]])

            HEADCFG = ((0, big1, (R["big1a"], R["big1b"]), R["ep0"], 0, 2 + j,
                        R["biasrep"], pv, R["pv"]),
                       (1, big2, (R["big2a"], R["big2b"]), R["ep1"], 1024, 10 + j,
                        R["biasrep2"], small, R["small"]))

            def emit_exp_h(cfg, grp):
                par, gbuf, gr, eres, ecol, bcol, brs = cfg[:7]
                wide = 512 * len(grp)
                diag = len(grp) == 2 and grp[0][0] >= 8 * k
                je = grp[0][0] - 8 * k if diag else 0
                if diag:
                    off0, off1 = 64 * je, 512 + 64 * (je + 1)
                    op("scalar", lambda e, ecol=ecol, gbuf=gbuf, bcol=bcol, off0=off0:
                       e.activation(ep[:, ecol + off0:ecol + 512], gbuf[:, off0:512],
                                    AF.Exp, bias=srep[:, bcol:bcol + 1]),
                       reads=[gr[0], brs], writes=[eres])
                    op("scalar", lambda e, ecol=ecol, gbuf=gbuf, bcol=bcol, off1=off1:
                       e.activation(ep[:, ecol + off1:ecol + 1024], gbuf[:, off1:1024],
                                    AF.Exp, bias=srep[:, bcol:bcol + 1]),
                       reads=[gr[1], brs], writes=[eres])
                else:
                    op("scalar", lambda e, ecol=ecol, wide=wide, gbuf=gbuf, bcol=bcol:
                       e.activation(ep[:, ecol:ecol + wide], gbuf[:, 0:wide], AF.Exp,
                                    bias=srep[:, bcol:bcol + 1]),
                       reads=[gr[0], gr[1], brs], writes=[eres])
                if diag:
                    off = ecol + 64 * je
                    mask_ap = bass.AP(ep, off, [[2048, 128], [512, 2], [1, 128]])
                    op("vector", lambda e, mask_ap=mask_ap: e.tensor_tensor(
                        mask_ap, mask_ap,
                        cbf[:, CB_MASK:CB_MASK + 256].rearrange("p (a w) -> p a w", a=2),
                        ALU.mult),
                       reads=[eres, R["cbf"]], writes=[eres])

            def emit_pv_h(cfg, grp):
                par, _, _, eres, ecol = cfg[:5]
                pvb, pvres = cfg[7], cfg[8]
                h = 2 * j + par
                for ci, (c, t0, vcix) in enumerate(grp):
                    op("tensor", lambda e, ci=ci, t0=t0, vcix=vcix, h=h, ecol=ecol,
                       pvb=pvb, first=(c == 0), last=(vcix == 32): e.matmul(
                        pvb[0:65, t0:512],
                        vc[:, vcix * VCH + h * 65: vcix * VCH + h * 65 + 65],
                        ep[:, ecol + ci * 512 + t0: ecol + ci * 512 + 512],
                        start=first, stop=last, skip_group_check=True),
                       reads=[R["vc"], eres], writes=[pvres])

            emit_qkt(groups[0])
            if pending_epi:
                for fn in pending_epi:
                    fn()
                pending_epi = []
            for g in range(len(groups)):
                if g > 0:
                    emit_pv_h(HEADCFG[0], groups[g - 1])
                emit_exp_h(HEADCFG[0], groups[g])
                if g > 0:
                    emit_pv_h(HEADCFG[1], groups[g - 1])
                emit_exp_h(HEADCFG[1], groups[g])
                if g + 1 < len(groups):
                    emit_qkt(groups[g + 1])
            emit_pv_h(HEADCFG[0], groups[-1])
            emit_pv_h(HEADCFG[1], groups[-1])

            for par, pvb, pvres, dslot, dres in (
                    (0, pv, R["pv"], dsb0, R["trigc"]),
                    (1, small, R["small"], dsb1, R["trigs"])):
                op("scalar", lambda e, pvb=pvb: e.activation(
                    hsb[64:65, 0:512], pvb[64:65, :], AF.Ln),
                   reads=[pvres], writes=[R["hsb"]])
                op("scalar", lambda e, dslot=dslot: e.activation(
                    dslot[64:65, :], hsb[64:65, 0:512], AF.Exp, scale=-1.0),
                   reads=[R["hsb"]], writes=[dres], late=True)
                if par == 0:
                    op("vector", lambda e, j=j, pvb=pvb: e.tensor_copy(
                        ao[0:64, j * 512: j * 512 + 512], pvb[0:64, :]),
                       reads=[pvres], writes=[R["xt"]])
                else:
                    op("vector", lambda e, pvb=pvb: e.tensor_copy(
                        qraw[0:64, 0:512], pvb[0:64, :]),
                       reads=[pvres], writes=[R["qraw0"]])
                    op("gpsimd", lambda e, j=j: e.dma_start(
                        ao[64:128, j * 512: j * 512 + 512], qraw[0:64, 0:512]),
                       reads=[R["qraw0"]], writes=[R["xt"]], dma=True)

            def _epi(j=j):
                op("tensor", lambda e: e.matmul(
                    misc[0:64, 0:512], cf32[64:65, 0:64], dsb0[64:65, :],
                    start=True, stop=True, skip_group_check=True),
                   reads=[R["cf32"], R["trigc"]], writes=[R["misc"]])
                op("tensor", lambda e: e.matmul(
                    misc[64:128, 0:512], cf32[64:65, 0:64], dsb1[64:65, :],
                    start=True, stop=True, skip_group_check=True),
                   reads=[R["cf32"], R["trigs"]], writes=[R["misc"]])
                op("vector", lambda e: e.tensor_copy(rcp_f32[:, 0:512], misc[:, 0:512]),
                   reads=[R["misc"]], writes=list(QRAW_ALL))
                op("vector", lambda e, j=j: e.tensor_mul(
                    ao[:, j * 512: j * 512 + 512], ao[:, j * 512: j * 512 + 512],
                    rcp_f32[:, 0:512]),
                   reads=list(QRAW_ALL), writes=[R["xt"]])
            if j == NCH - 1:
                _epi()
            else:
                pending_epi.append(_epi)

        if dbg and k == 0:
            op("gpsimd", lambda e: e.dma_start(dbg_ao[:], ao[:]), reads=[R["xt"]], dma=True)
        op("sync", lambda e, trow=k * 512: e.dma_start(xres[:], x_res_d[trow:trow + 128, :]),
           reads=[R["xres"]], writes=[R["xres"]], dma=True)
        for tsub in range(4):
            trow = k * 512 + tsub * 128
            for nh in range(2):
                for cc in range(NCH):
                    op("tensor", lambda e, cc=cc, nh=nh, tsub=tsub, wo_buf=wo_buf: e.matmul(
                        big1[:, nh * 512: nh * 512 + 512],
                        ao[:, cc * 512 + tsub * 128: cc * 512 + tsub * 128 + 128],
                        wo_buf[:, cc * 1024 + nh * 512: cc * 1024 + nh * 512 + 512],
                        start=(cc == 0), stop=(cc == NCH - 1), skip_group_check=True),
                       reads=[R["xt"], wo_res],
                       writes=[R["big1a"] if nh == 0 else R["big1b"]])
            op("vector", lambda e: e.tensor_add(rsb[:], big1[:, 0:1024], xres[:]),
               reads=[R["big1a"], R["big1b"], R["xres"]], writes=[R["rsb"]])
            emit_warm(2)
            emit_ln(R["rsb"], rsb[:], [R["hsb"]], hsb[:])
            emit_warm(2)
            op("vector", lambda e: e.tensor_scalar(rsb[:], hsb[:], srep[:, 0:1], FP8MAX,
                                                   ALU.mult, ALU.min),
               reads=[R["hsb"], R["srep"]], writes=[R["rsb"]])
            op("vector", lambda e: e.tensor_scalar_max(rsb[:], rsb[:], -FP8MAX),
               reads=[R["rsb"]], writes=[R["rsb"]])
            op("vector", lambda e: e.tensor_scalar_mul(xres[:], rsb[:], VELT),
               reads=[R["rsb"]], writes=[R["xres"]])
            op("gpsimd", lambda e: e.tensor_sub(osb[:], xres[:], rsb[:]),
               reads=[R["xres"], R["rsb"]], writes=list(EP_ALL))
            op("vector", lambda e: e.tensor_sub(hq, xres[:], osb[:]),
               reads=[R["xres"]] + list(EP_ALL), writes=[R["qsb"]])
            if tsub < 3:
                op("sync", lambda e, trow=trow + 128: e.dma_start(
                    xres[:], x_res_d[trow:trow + 128, :]),
                   reads=[R["xres"]], writes=[R["xres"]], dma=True)
            for cc in range(NCH):
                sl = (cc % 2) * 128
                op("tensor", lambda e, cc=cc, sl=sl: e.transpose(
                    tp[:, sl:sl + 128], hq[:, cc * 128: cc * 128 + 128],
                    cbf[:, CB_ID:CB_ID + 128]),
                   reads=[R["qsb"], R["cbf"]], writes=[R["tp"]])
                op("vector", lambda e, cc=cc, sl=sl: e.tensor_copy(
                    hqT[:, cc * 128: cc * 128 + 128], tp[:, sl:sl + 128]),
                   reads=[R["tp"]], writes=[R["qsb"]])
            for nh in range(2):
                op("tensor", lambda e, nh=nh: e.matmul(
                    big2[:, nh * 512: nh * 512 + 512], cbf[0:1, CB_ONEROW:CB_ONEROW + 128],
                    cbf[0:1, CB_BIAS + nh * 512: CB_BIAS + nh * 512 + 512],
                    start=True, stop=False, skip_group_check=True),
                   reads=[R["cbf"]], writes=[R["big2a"] if nh == 0 else R["big2b"]])
                for cc in range(NCH):
                    op("tensor", lambda e, cc=cc, nh=nh, wq_buf=wq_buf: e.matmul(
                        big2[:, nh * 512: nh * 512 + 512],
                        hqT[:, cc * 128: cc * 128 + 128],
                        wq_buf[:, cc * 1024 + nh * 512: cc * 1024 + nh * 512 + 512],
                        start=False, stop=(cc == NCH - 1), skip_group_check=True),
                       reads=[R["qsb"], wq_res],
                       writes=[R["big2a"] if nh == 0 else R["big2b"]])
            op("vector", lambda e: e.scalar_tensor_tensor(
                rsb[:], big2[:, 0:1024], srep[:, 1:2], hsb[:], ALU.mult, ALU.add),
               reads=[R["big2a"], R["big2b"], R["srep"], R["hsb"]], writes=[R["rsb"]])
            emit_warm(2)
            emit_ln(R["rsb"], rsb[:], EP_ALL, osb[:])
            emit_warm(2)
            op("sync", lambda e, trow=trow: e.dma_start(out_d[trow:trow + 128, :], osb[:]),
               reads=list(EP_ALL), dma=True)

        if k < NQ - 1:
            op("gpsimd", lambda e, wo_buf=wo_buf: e.dma_start(wo_buf[:].rearrange("p (a d) -> p a d", a=8), wq_d[:].rearrange("(a p) d -> p a d", p=128)),
               reads=[wo_res], writes=[wo_res], dma=True)
            op("gpsimd", lambda e, wq_buf=wq_buf: e.dma_start(wq_buf[:].rearrange("p (a d) -> p a d", a=8), wout_d[:].rearrange("(a p) d -> p a d", p=128)),
               reads=[wq_res], writes=[wq_res], dma=True)

    ENGS = ("sync", "tensor", "scalar", "vector", "gpsimd")
    for o in ops:
        if o["dma"]:
            o["need_inc"] = True
        for d in o["deps"]:
            dd = ops[d]
            if dd["dma"] or dd["eng"] != o["eng"] or dd["late"]:
                dd["need_inc"] = True
    sem_of = {}
    for e in ENGS:
        sem_of[(e, False)] = "c_" + e
        sem_of[(e, True)] = "d_" + e
    sems = {nm: es.enter_context(nc.semaphore(nm)) for nm in set(sem_of.values())}
    counts = {nm: 0 for nm in sems}
    for o in ops:
        if o["need_inc"]:
            nm = sem_of[(o["eng"], o["dma"])]
            counts[nm] += 16 if o["dma"] else 1
            o["sem"] = nm
            o["val"] = counts[nm]
    finals = dict(counts)

    def emit_for(eng_name):
        def run(eng):
            w = {}
            for o in ops:
                if o["eng"] != eng_name:
                    continue
                for d in sorted(o["deps"]):
                    dd = ops[d]
                    if not (dd["dma"] or dd["eng"] != eng_name or dd["late"]):
                        continue
                    if dd["sem"] is None:
                        continue
                    if w.get(dd["sem"], 0) < dd["val"]:
                        eng.wait_ge(sems[dd["sem"]], dd["val"])
                        w[dd["sem"]] = dd["val"]
                inst = o["build"](eng)
                if o["need_inc"]:
                    inst.then_inc(sems[o["sem"]], 16 if o["dma"] else 1)
                if o["dma"]:
                    eng.wait_ge(sems[o["sem"]], o["val"])
                    w[o["sem"]] = max(w.get(o["sem"], 0), o["val"])
            if eng_name == "sync":
                for nm, v in finals.items():
                    if v > 0:
                        eng.wait_ge(sems[nm], v)
        return run

    with nc.Block() as blk:
        blk.sync(emit_for("sync"))
        blk.tensor(emit_for("tensor"))
        blk.scalar(emit_for("scalar"))
        blk.vector(emit_for("vector"))
        blk.gpsimd(emit_for("gpsimd"))

    return nc



def _perm():
    p = np.zeros(C, dtype=np.int64)
    for h in range(H):
        for m in range(D):
            p[h * D + m] = h * D + (2 * m if m < 32 else 2 * (m - 32) + 1)
    return p


def _rope_tables(pos, scale):
    i = np.arange(32, dtype=np.float64)
    freqs = 1.0 / (ROPE_THETA ** (2.0 * i / D))
    ang = freqs[:, None] * np.asarray(pos, np.float64)[None, :]
    cos32 = np.cos(ang) * scale
    sin32 = np.sin(ang) * scale
    cos = np.concatenate([cos32, cos32, cos32, cos32], axis=0)
    sin = np.concatenate([-sin32, sin32, -sin32, sin32], axis=0)
    return np.ascontiguousarray(cos.astype(np.float32)), np.ascontiguousarray(sin.astype(np.float32))


def _consts(pi):
    cbf = np.zeros((128, NCBF), dtype=np.float32)
    for m in range(128):
        sw = m + 32 if (m % 64) < 32 else m - 32
        cbf[sw, CB_PMAT + m] = 1.0
    cbf[:, CB_ID:CB_ID + 128] = np.eye(128, dtype=np.float32)
    u = np.arange(128)
    for q in range(256):
        half, qq = q // 128, (q % 128) // 16
        lim = 32 * qq + 16 * pi + (16 if half == 0 else -112)
        cbf[:, CB_MASK + q] = (u < lim).astype(np.float32)
    cbf[0:64, CB_ONE_U] = 1.0
    cbf[64:128, CB_ONE_L] = 1.0
    cbf[0, CB_ONEROW:CB_ONEROW + 128] = 1.0
    cf32 = np.ones((128, NCF), dtype=np.float32)
    return np.ascontiguousarray(cbf.astype(BFNP)), cf32


def kernel(L0_attn_x, L0_attn_Wq, L0_attn_Wk, L0_attn_Wv, L0_attn_lat_k,
           L0_attn_lat_v, L0_attn_W_lk, L0_attn_W_lv, L0_attn_W_out,
           L0_ffn_weight_fp8, L0_ffn_bias, L0_ffn_input_scale,
           L0_ffn_weight_scale):
    x = np.asarray(L0_attn_x, dtype=np.float32)
    perm = _perm()
    wq = np.ascontiguousarray(np.asarray(L0_attn_Wq, np.float32)[:, perm])
    wk = np.ascontiguousarray(np.asarray(L0_attn_Wk, np.float32)[:, perm])
    wv = np.ascontiguousarray(np.asarray(L0_attn_Wv, np.float32))
    wout = np.ascontiguousarray(np.asarray(L0_attn_W_out, np.float32))
    wffn = np.ascontiguousarray(np.asarray(L0_ffn_weight_fp8, np.float32).T)
    wlk = np.ascontiguousarray(np.asarray(L0_attn_W_lk, np.float32)[:, perm])
    wlv = np.ascontiguousarray(np.asarray(L0_attn_W_lv, np.float32))
    latk = np.ascontiguousarray(np.asarray(L0_attn_lat_k, np.float32)[0].T)
    latv = np.ascontiguousarray(np.asarray(L0_attn_lat_v, np.float32)[0].T)
    bias = np.ascontiguousarray(np.asarray(L0_ffn_bias, np.float32).reshape(1, C))
    scl = np.array([[float(np.asarray(L0_ffn_input_scale).reshape(-1)[0]),
                     float(np.asarray(L0_ffn_weight_scale).reshape(-1)[0])]],
                   dtype=np.float32)
    cos_kv, sin_kv = _rope_tables(np.arange(T), 1.0)

    in_maps = []
    row_sets = []
    for c in range(8):
        b, pi = c // 2, c % 2
        rows = np.arange(T)
        rows = rows[(rows // PG) % 2 == pi]
        row_sets.append(rows)
        own = np.ascontiguousarray(x[b][rows])
        cos_q, sin_q = _rope_tables(rows, 1.0 / np.sqrt(D))
        cbf_np, cf32_np = _consts(pi)
        in_maps.append({
            "xt_kv": np.ascontiguousarray(x[b].T),
            "xt_q": np.ascontiguousarray(own.T),
            "x_res": own,
            "wq": wq, "wk": wk, "wv": wv, "wout": wout, "wffn": wffn,
            "wlk": wlk, "wlv": wlv, "latk": latk, "latv": latv,
            "bias": bias, "scl": scl,
            "cos_kv": cos_kv, "sin_kv": sin_kv,
            "cos_q": cos_q, "sin_q": sin_q,
            "cbf": cbf_np, "cf32": cf32_np,
        })

    nc = build_nc()
    res = run_bass_kernel_spmd(nc, in_maps, core_ids=list(range(8)))
    out = np.empty((B, T, C), dtype=np.float32)
    for c in range(8):
        out[c // 2][row_sets[c]] = np.asarray(res.results[c]["out"], dtype=np.float32)
    return out
